# version 02 (final)
# baseline (speedup 1.0000x reference)
import os
import numpy as np
import ml_dtypes
import concourse.bass as bass
import concourse.mybir as mybir
from concourse.bass_utils import run_bass_kernel_spmd
from concourse.alu_op_type import AluOpType as OP
from contextlib import ExitStack

F32 = mybir.dt.float32
BF16 = mybir.dt.bfloat16
AF = mybir.ActivationFunctionType

D, TOK, J, H = 2048, 2048, 6144, 8192
DC, JC, HC = 16, 48, 64
NG = 512
G = TOK // NG
EPS = 1e-5
C_FWHT = float(np.float32(1.0 / np.sqrt(8192.0)))


def _consts():
    import jax
    with jax.default_device(jax.devices("cpu")[0]):
        import jax.numpy as jnp
        rot = D // 2
        inv_freq = 1.0 / 10000.0 ** (jnp.arange(0, rot, 2, dtype=jnp.float32) / rot)
        fr = jnp.arange(TOK, dtype=jnp.float32)[:, None] * inv_freq[None, :]
        fr = jnp.repeat(fr, 2, axis=-1)
        cos = np.asarray(jnp.cos(fr), dtype=np.float32)
        sin = np.asarray(jnp.sin(fr), dtype=np.float32)
    cosfm = np.ascontiguousarray(cos.T.reshape(8, 128, TOK))
    sinfm = np.ascontiguousarray(sin.T.reshape(8, 128, TOK))
    h = np.array([[1.0]])
    while h.shape[0] < 128:
        h = np.block([[h, h], [h, -h]])
    h128 = h.astype(ml_dtypes.bfloat16)
    h64b = np.zeros((128, 128))
    h64b[:64, :64] = h[:64, :64]
    h64b[64:, 64:] = h[:64, :64]
    h64b = h64b.astype(ml_dtypes.bfloat16)
    r = np.zeros((128, 128))
    for k in range(64):
        r[2 * k, 2 * k + 1] = -1.0
        r[2 * k + 1, 2 * k] = 1.0
    rmt = np.ascontiguousarray(r.T).astype(ml_dtypes.bfloat16)
    return cosfm, sinfm, h128, h64b, rmt


def build_nc():
    nc = bass.Bass("TRN2", target_bir_lowering=False)
    dt = nc.dram_tensor
    xh_h = dt("xh", [D, TOK], F32, kind="ExternalInput")
    win_h = dt("win", [D, J], F32, kind="ExternalInput")
    woutb_h = dt("woutb", [DC, 128, DC, 128], F32, kind="ExternalInput")
    w1t_h = dt("w1t", [HC, 128, DC, 128], F32, kind="ExternalInput")
    w3t_h = dt("w3t", [HC, 128, DC, 128], F32, kind="ExternalInput")
    w2t_h = dt("w2t", [H, D], F32, kind="ExternalInput")
    cos_h = dt("cosfm", [8, 128, TOK], F32, kind="ExternalInput")
    sin_h = dt("sinfm", [8, 128, TOK], F32, kind="ExternalInput")
    lnwb_h = dt("lnwb", [128, DC, 2], F32, kind="ExternalInput")
    bin_h = dt("binrow", [1, J], F32, kind="ExternalInput")
    bout_h = dt("boutrow", [1, D], F32, kind="ExternalInput")
    h128_h = dt("h128c", [128, 128], BF16, kind="ExternalInput")
    h64b_h = dt("h64bc", [128, 128], BF16, kind="ExternalInput")
    rmt_h = dt("rmtc", [128, 128], BF16, kind="ExternalInput")
    on_h = dt("on128", [128, 1], BF16, kind="ExternalInput")
    or_h = dt("onrow", [1, 128], F32, kind="ExternalInput")
    y_h = dt("y", [D, TOK], F32, kind="ExternalOutput")
    KDBG = os.environ.get("KDBG", "")
    kw = dict(kind="ExternalOutput") if KDBG else {}
    winall_h = dt("winall", [JC * 128 * DC * 128], BF16, **kw)
    vscr_h = dt("vscr", [H, D], BF16, **kw)
    w2h_h = dt("w2h", [DC * 128 * HC * 128], BF16, **kw)
    sscr_h = dt("sscr", [DC, 128, TOK], BF16, **kw)
    dbgw_h = dt("dbgw", [G, DC, 128, DC * 128], BF16, **kw)
    rows_h = dt("rowscr", [4, TOK], F32, **kw)
    rowsb_h = dt("rowbscr", [4, TOK], BF16, **kw)
    KPH = int(os.environ.get("KPH", "9"))

    lnwb, binrow, boutrow = lnwb_h.ap(), bin_h.ap(), bout_h.ap()
    h128c, h64bc, rmtc = h128_h.ap(), h64b_h.ap(), rmt_h.ap()
    on128, onrow, y = on_h.ap(), or_h.ap(), y_h.ap()
    sscr, rows, rowsb, w2t = sscr_h.ap(), rows_h.ap(), rowsb_h.ap(), w2t_h.ap()
    vscr = vscr_h.ap()

    stk = ExitStack()
    ec = stk.enter_context
    sb = nc.sbuf_tensor
    z = ec(sb("z", [128, HC, NG], BF16))
    Ta = ec(sb("Ta", [128, DC, NG], BF16))
    Tb = ec(sb("Tb", [128, DC, NG], BF16))
    Tc = ec(sb("Tc", [128, DC, NG], BF16))
    cost = ec(sb("cost", [128, 8, NG], BF16))
    sint = ec(sb("sint", [128, 8, NG], BF16))
    t1 = ec(sb("t1", [128, NG], F32))
    t2 = ec(sb("t2", [128, NG], F32))
    vnb = ec(sb("vnb", [128, NG], F32))
    bc1 = ec(sb("bc1", [128, NG], F32))
    bc2 = ec(sb("bc2", [128, NG], F32))
    xsq = ec(sb("xsq", [128, 2, NG], BF16))
    qt = ec(sb("qt", [128, 2, NG], BF16))
    xbc = ec(sb("xbc", [128, 2, NG], BF16))
    xf = ec(sb("xf", [128, 2, NG], F32))
    gt = ec(sb("gt", [128, NG], F32))
    fold = ec(sb("fold", [128, 1, 4, 128], BF16))
    w2tin = ec(sb("w2tin", [128, D], BF16))
    ptst = ec(sb("ptst", [128, 1024], BF16))
    w2hs = ec(sb("w2hs", [128, DC, 128], BF16))
    rCB = ec(sb("rCB", [1, J], BF16))
    rCW = ec(sb("rCW", [1, J - D], BF16))
    rF = ec(sb("rF", [1, 7 * NG], F32))
    rB = ec(sb("rB", [1, 5 * NG], BF16))
    boutbf = ec(sb("boutbf", [1, D], BF16))
    binc = ec(sb("binc", [1, NG], F32))
    lnf = ec(sb("lnf", [128, DC, 2], F32))
    lnb16 = ec(sb("lnb16", [128, DC, 2], BF16))
    o128 = ec(sb("o128", [128, 1], BF16))
    orow = ec(sb("orow", [1, 128], F32))
    hh128 = ec(sb("hh128", [128, 128], BF16))
    hh64 = ec(sb("hh64", [128, 128], BF16))
    rmts = ec(sb("rmts", [128, 128], BF16))
    zrow = ec(sb("zrow", [1, 16], BF16))
    Af = ec(sb("Af", [128, DC], F32))
    Ap = ec(sb("Ap", [128, DC], F32))

    pA = ec(nc.psum_tensor("pA", [128, 512], F32))
    pB = ec(nc.psum_tensor("pB", [128, 512], F32))
    pC = ec(nc.psum_tensor("pC", [128, 512], F32))
    pD = ec(nc.psum_tensor("pD", [128, 512], F32))
    pEF = ec(nc.psum_tensor("pEF", [128, 1024], F32))
    pS1 = ec(nc.psum_tensor("pS1", [1, 512], F32))
    pS2 = ec(nc.psum_tensor("pS2", [1, 512], F32))

    sT = ec(nc.semaphore("semT"))
    sV = ec(nc.semaphore("semV"))
    sS = ec(nc.semaphore("semS"))
    sDS = ec(nc.semaphore("semDS"))
    sDG = ec(nc.semaphore("semDG"))
    sO = ec(nc.semaphore("semO"))
    SEM = {"T": sT, "V": sV, "S": sS, "DS": sDS, "DG": sDG, "O": sO}
    block = ec(nc.Block())

    KSYNC = int(os.environ.get("KSYNC", "5"))

    def sched(cur, eng):
        c = {k: 0 for k in SEM}

        def mk(name):
            def emit(f, inc=1):
                if KSYNC >= 5 and cur == name:
                    for s in ("T", "V", "S", "DS", "DG", "O"):
                        if s != name and c[s]:
                            eng.wait_ge(SEM[s], c[s])
                r = f(eng) if cur == name else None
                if inc:
                    c[name] += inc
                    if r is not None:
                        r.then_inc(SEM[name], inc)
                return (name, c[name])
            return emit
        T_, V_, S_ = mk("T"), mk("V"), mk("S")
        T = lambda f, inc=1: T_(f, inc)[1]
        V = lambda f, inc=1: V_(f, inc)[1]
        S = lambda f, inc=1: S_(f, inc)[1]

        def DS(f):
            if cur == "SY":
                if KSYNC >= 5:
                    for s in ("T", "V", "S", "DG", "O"):
                        if c[s]:
                            eng.wait_ge(SEM[s], c[s])
                f(eng).then_inc(sDS, 16)
            c["DS"] += 16
            return c["DS"]

        def DG(f):
            if cur == "G":
                if KSYNC >= 5:
                    for s in ("T", "V", "S", "DS", "O"):
                        if c[s]:
                            eng.wait_ge(SEM[s], c[s])
                f(eng).then_inc(sDG, 16)
            c["DG"] += 16
            return c["DG"]

        def DO(f):
            if cur == "SY":
                if KSYNC >= 5:
                    for s in ("T", "V", "S", "DG"):
                        if c[s]:
                            eng.wait_ge(SEM[s], c[s])
                f(eng).then_inc(sO, 16)
            c["O"] += 16
            return c["O"]

        def w(engine, sem, val):
            if val and cur == engine:
                eng.wait_ge(SEM[sem], val)

        def wf(engine, tok):
            if tok is not None and tok[1]:
                w(engine, tok[0], tok[1])

        Z = ("T", 0)
        MM = lambda out, l, r, st_, sp: (lambda e: e.matmul(out, l, r, start=st_, stop=sp))
        TT = lambda out, a, b, op: (lambda e: e.tensor_tensor(out, a, b, op))
        TSc = lambda out, a, s1, op: (lambda e: e.tensor_scalar(out, a, s1, None, op))
        CP = lambda out, a: (lambda e: e.tensor_copy(out, a))
        RC = lambda out, a: (lambda e: e.reciprocal(out, a))
        ACT = lambda out, a, f, **kw: (lambda e: e.activation(out, a, f, **kw))
        RR = lambda i: rF[0:1, i * NG:(i + 1) * NG]
        BB = lambda i: rB[0:1, i * NG:(i + 1) * NG]

        bank = {"pA": Z, "pB": Z, "pC": Z, "pD": Z, "pEF0": Z, "pEF1": Z, "pS1": Z, "pS2": Z}
        def barrier(level):
            if KSYNC >= level:
                for s in ("T", "V", "S", "DS", "DG", "O"):
                    w(cur, s, c[s])

        def bcast(srcrow, dstb, wv=0):
            vh = V(CP(rB[0:1, 3 * NG:4 * NG], srcrow), 0)
            vl = V(TT(rB[0:1, 4 * NG:5 * NG], srcrow, rB[0:1, 3 * NG:4 * NG], OP.subtract))
            wf("T", bank["pD"])
            w("T", "V", vl)
            if wv:
                w("T", "V", wv)
            T(MM(pD[:, :], hh128[0:1, :], rB[0:1, 3 * NG:4 * NG], True, False), 0)
            tb_ = T(MM(pD[:, :], hh128[0:1, :], rB[0:1, 4 * NG:5 * NG], False, True))
            w("V", "T", tb_)
            bank["pD"] = V_(CP(dstb[:, :], pD[:, :]))
            return tb_, bank["pD"][1]

        d0 = 0
        for dst, src in [(lnf[:, :, :], lnwb[:, :, :]), (o128[:, :], on128[:, :]),
                         (orow[:, :], onrow[:, :]), (hh128[:, :], h128c[:, :]),
                         (hh64[:, :], h64bc[:, :]), (rmts[:, :], rmtc[:, :]),
                         (rF[0:1, 0:D], boutrow[:, :])]:
            d0 = DS(lambda e, dst=dst, src=src: e.dma_start(dst, src))
        w("V", "DS", d0)
        V(lambda e: e.memset(zrow[0:1, :], 0.0), 0)
        V(lambda e: e.memset(Af[:, :], 0.0), 0)
        V(CP(lnb16[:, :, :], lnf[:, :, :]), 0)
        v_consts = V(CP(boutbf[:, :], rF[0:1, 0:D]))

        pan = [Ta, Tb]
        pan_free = {}
        binc_free = {0: 0, 1: 0}
        fold_free = {0: 0, 1: 0}
        last_store = 0
        for jj in range(12):
            buf = pan[jj % 2]
            prev = pan_free.get(jj - 2)
            if prev is not None:
                w("G", "T", prev[0])
                w("G", "S", prev[1])
            dref = DG(lambda e, jj=jj, buf=buf: e.dma_start(
                buf[:, :, :], bass.AP(win_h, jj * NG, [[J, 128], [128 * J, DC], [1, NG]])))
            w("SY", "V", binc_free[0])
            dbin = DS(lambda e, jj=jj: e.dma_start(binc[0:1, :],
                                                   binrow[0:1, jj * NG:(jj + 1) * NG]))
            w("T", "DG", dref)
            if jj == 0:
                w("T", "V", v_consts)
            wf("T", bank["pS1"])
            for dc in range(DC):
                tcb = T(MM(pS1[0:1, :], lnb16[:, dc, 0:1], buf[:, dc, :], dc == 0, dc == DC - 1),
                        1 if dc == DC - 1 else 0)
            tcw = tcb
            if jj >= 4:
                wf("T", bank["pS2"])
                for dc in range(DC):
                    tcw = T(MM(pS2[0:1, :], lnb16[:, dc, 1:2], buf[:, dc, :], dc == 0, dc == DC - 1),
                            1 if dc == DC - 1 else 0)
            w("V", "T", tcw)
            w("V", "DS", dbin)
            bank["pS1"] = V_(TT(rCB[0:1, jj * NG:(jj + 1) * NG], pS1[0:1, :],
                                binc[0:1, :], OP.add))
            binc_free[0] = bank["pS1"][1]
            if jj >= 4:
                bank["pS2"] = V_(CP(rCW[0:1, (jj - 4) * NG:(jj - 3) * NG], pS2[0:1, :]))
            for dc in range(DC):
                fs = 0
                w("S", "DS", fold_free[fs])
                w("S", "DS", d0 if jj == 0 and dc == 0 else 0)
                w("S", "DG", dref if dc == 0 else 0)
                fv = S(ACT(fold[:, fs, :, :], buf[:, dc, :], AF.Identity, scale=lnf[:, dc, 1:2]))
                w("SY", "S", fv)
                off = jj * 4 * (128 * DC * 128) + dc * 128
                fold_free[fs] = last_store = DS(lambda e, fs=fs, off=off: e.dma_start(
                    bass.AP(winall_h, off, [[DC * 128, 128], [128 * DC * 128, 4], [1, 128]]),
                    fold[:, fs, :, :]))
            pan_free[jj] = (tcw, fv)
            barrier(4)
        p1_stores = last_store
        barrier(1)

        def _dump_exit():
            vz = V(lambda e: e.memset(t1[:, :], 0.0))
            w("SY", "V", vz)
            w("SY", "DS", c["DS"])
            oc = DO(lambda e: e.dma_start(y[0:128, 0:NG], t1[:, :]))
            w("SY", "O", oc)
            return c
        if KPH <= 1:
            return _dump_exit()

        PT = {"w2tin_free_T": 0, "ptst_free_DS": 0, "w2hs_free_DS": 0, "done": 0,
              "pta_last_store": 0}

        def pta_step(cstep):
            w("G", "T", PT["w2tin_free_T"])
            dref = DG(lambda e, cstep=cstep: e.dma_start(
                w2tin[:, :], w2t[cstep * 128:(cstep + 1) * 128, :]))
            w("T", "DG", dref)
            for hq in range(2):
                wf("T", bank["pEF0"])
                wf("T", bank["pEF1"])
                for q in range(2):
                    qq = hq * 2 + q
                    tl = T(MM(pEF[:, q * 512:(q + 1) * 512], hh128[:, :],
                              w2tin[:, qq * 512:(qq + 1) * 512], True, True),
                           1 if q == 1 else 0)
                if hq == 1:
                    PT["w2tin_free_T"] = tl
                w("S", "T", tl)
                w("S", "DS", PT["ptst_free_DS"])
                sev = S_(ACT(ptst[:, :], pEF[:, :], AF.Identity))
                bank["pEF0"] = sev
                bank["pEF1"] = sev
                w("SY", "S", sev[1])
                PT["ptst_free_DS"] = DS(lambda e, cstep=cstep, hq=hq: e.dma_start(
                    vscr[cstep * 128:(cstep + 1) * 128, hq * 1024:(hq + 1) * 1024], ptst[:, :]))
            PT["pta_last_store"] = PT["ptst_free_DS"]
            barrier(4)

        def ptb_step(k):
            a, b = 2 * k, 2 * k + 1
            if k == 0:
                w("G", "DS", PT["pta_last_store"])
            w("G", "T", PT["w2tin_free_T"])
            d1 = DG(lambda e, a=a: e.dma_start(
                w2tin[0:64, :], bass.AP(vscr_h, a * D, [[128 * D, 64], [1, D]])))
            d1 = DG(lambda e, b=b: e.dma_start(
                w2tin[64:128, :], bass.AP(vscr_h, b * D, [[128 * D, 64], [1, D]])))
            w("T", "DG", d1)
            sev = ("S", 0)
            for q in range(4):
                bk = "pEF0" if q % 2 == 0 else "pEF1"
                wf("T", bank[bk])
                tl = T(MM(pEF[:, (q % 2) * 512:(q % 2 + 1) * 512], hh64[:, :],
                          w2tin[:, q * 512:(q + 1) * 512], True, True))
                if q == 3:
                    PT["w2tin_free_T"] = tl
                w("S", "T", tl)
                if q == 0:
                    w("S", "DS", PT["w2hs_free_DS"])
                sev = S_(ACT(w2hs[:, q * 4:(q + 1) * 4, :],
                             pEF[:, (q % 2) * 512:(q % 2 + 1) * 512],
                             AF.Identity, scale=C_FWHT))
                bank[bk] = sev
            last = 0
            for half, pp in ((0, a), (1, b)):
                for q in range(4):
                    w("SY", "S", sev[1])
                    off = pp * (HC * 128) + q * 4 * (128 * HC * 128)
                    last = DS(lambda e, half=half, q=q, off=off: e.dma_start(
                        bass.AP(w2h_h, off, [[128, 64], [128 * HC * 128, 4], [1, 128]]),
                        w2hs[half * 64:(half + 1) * 64, q * 4:(q + 1) * 4, :]))
            PT["w2hs_free_DS"] = last
            PT["done"] = last
            barrier(4)

        ta_free = pan_free[10]
        ta_free_dom2 = "S"
        tb_first = pan_free[11]
        blk_use = {s: 0 for s in range(4)}
        xsq_free = {s: 0 for s in range(2)}
        t1_free_S = 0
        ap_free_V = 0
        cos_free_V = 0
        rows_stored = {}
        tb_free_V = 0
        rf_free = (("T", 0), ("V", 0))
        for g in range(G):
            gl = slice(g * NG, (g + 1) * NG)
            w("G", "T", ta_free[0])
            w("G", ta_free_dom2, ta_free[1])
            dx = DG(lambda e, g=g: e.dma_start(
                Ta[:, :, :], bass.AP(xh_h, g * NG, [[TOK, 128], [128 * TOK, DC], [1, NG]])))
            w("G", "V", cos_free_V)
            dcs = DG(lambda e, g=g: e.dma_start(
                cost[:, :, :], bass.AP(cos_h, g * NG, [[TOK, 128], [128 * TOK, 8], [1, NG]])))
            dsn = DG(lambda e, g=g: e.dma_start(
                sint[:, :, :], bass.AP(sin_h, g * NG, [[TOK, 128], [128 * TOK, 8], [1, NG]])))
            w("T", "DG", dx)
            wf("T", bank["pS1"])
            for dc in range(DC):
                tsx = T(MM(pS1[0:1, :], o128[:, 0:1], Ta[:, dc, :], dc == 0, dc == DC - 1),
                        1 if dc == DC - 1 else 0)
            wf("T", bank["pS2"])
            for dc in range(DC):
                w("S", "DG", dx if dc == 0 else 0)
                w("S", "T", xsq_free[dc % 2])
                ssq = S(ACT(xsq[:, dc % 2, :], Ta[:, dc, :], AF.Square))
                w("T", "S", ssq)
                xsq_free[dc % 2] = T(MM(pS2[0:1, :], o128[:, 0:1], xsq[:, dc % 2, :],
                                        dc == 0, dc == DC - 1))
            tsq = xsq_free[(DC - 1) % 2]
            wf("V", rf_free[0])
            w("V", "T", tsq)
            V(TSc(RR(0), pS1[0:1, :], 1.0 / D, OP.mult), 0)
            bank["pS1"] = bank["pS2"] = V_(TSc(RR(1), pS2[0:1, :], 1.0 / D, OP.mult))
            V(TT(RR(5), RR(0), RR(0), OP.mult), 0)
            V(TT(RR(1), RR(1), RR(5), OP.subtract), 0)
            vv = V(TSc(RR(5), RR(1), EPS, OP.add))
            w("S", "V", vv)
            sq1 = S(ACT(RR(6), RR(5), AF.Sqrt))
            w("V", "S", sq1)
            V(RC(RR(2), RR(6)), 0)
            V(CP(BB(0), RR(6)), 0)
            V(RC(RR(6), RR(5)), 0)
            V(TT(RR(6), RR(1), RR(6), OP.mult), 0)
            vv = V(TSc(RR(6), RR(6), EPS, OP.add))
            w("S", "V", vv)
            sq2 = S(ACT(RR(6), RR(6), AF.Sqrt))
            w("V", "S", sq2)
            V(RC(RR(3), RR(6)), 0)
            V(TT(RR(4), RR(2), RR(3), OP.mult), 0)
            V(RC(RR(6), RR(4)), 0)
            V(CP(BB(1), RR(6)), 0)
            V(TSc(RR(6), RR(0), -1.0, OP.mult), 0)
            vrows = V(CP(BB(2), RR(6)))
            w("SY", "V", vrows)
            DS(lambda e, gl=gl: e.dma_start(rows[0:1, gl], RR(0)))
            DS(lambda e, gl=gl: e.dma_start(rows[1:2, gl], RR(2)))
            DS(lambda e, gl=gl: e.dma_start(rows[2:3, gl], RR(3)))
            DS(lambda e, gl=gl: e.dma_start(rowsb[0:1, gl], BB(0)))
            DS(lambda e, gl=gl: e.dma_start(rowsb[1:2, gl], BB(1)))
            rows_stored[g] = DS(lambda e, gl=gl: e.dma_start(rowsb[2:3, gl], BB(2)))
            if KPH == 20:
                return _dump_exit()
            barrier(3)
            kev = {}
            for jx in range(DC):
                jc = 16 + jx
                slot = jx % 4
                w("SY", "T", blk_use[slot])
                if g == 0 and jx < 4:
                    w("SY", "DS", p1_stores)
                dblk = DS(lambda e, jc=jc, slot=slot: e.dma_start(
                    Tc[:, :, slot * 128:(slot + 1) * 128],
                    bass.AP(winall_h, jc * 128 * DC * 128, [[DC * 128, 128], [128, DC], [1, 128]])))
                pk, bn = ([pA, "pA"], [pB, "pB"])[jx % 2]
                w("T", "DS", dblk)
                wf("T", bank[bn])
                if jx == 0:
                    w("T", "V", vrows)
                for dc in range(DC):
                    T(MM(pk[:, :], Tc[:, dc, slot * 128:(slot + 1) * 128], Ta[:, dc, :],
                         dc == 0, False), 0)
                T(MM(pk[:, :], rCB[0:1, jc * 128:(jc + 1) * 128], BB(1)[:, :], False, False), 0)
                blk_use[slot] = T(MM(pk[:, :], rCW[0:1, (jc - 16) * 128:(jc - 15) * 128], BB(2)[:, :],
                                     False, True))
                w("S", "T", blk_use[slot])
                if jx == 0:
                    w("S", "V", tb_free_V)
                    if g == 0:
                        w("S", "T", tb_first[0])
                bank[bn] = kev[jx] = S_(ACT(Tb[:, jx, :], pk[:, :], AF.Identity))
                barrier(4)
            barrier(3)
            if KPH == 21:
                return _dump_exit()
            wf("T", bank["pS1"])
            KROPE = int(os.environ.get("KROPE", "1"))
            for ch in range(DC):
                if ch < 8 and KROPE:
                    w("T", "S", kev[ch][1])
                    wf("T", bank["pC"])
                    trh = T(MM(pC[:, :], rmts[:, :], Tb[:, ch, :], True, True)) if KROPE in (1, 2, 3, 4) else 0
                    w("V", "T", trh)
                    if ch == 0:
                        w("V", "DG", dcs)
                        w("V", "DG", dsn)
                    w("V", "S", t1_free_S)
                    if KROPE in (1, 3, 4):
                        V(TT(t1[:, :], Tb[:, ch, :], cost[:, ch, :], OP.mult), 0)
                    else:
                        V(lambda e: e.memset(t1[:, :], 0.0), 0)
                    if KROPE in (1, 4):
                        bank["pC"] = V_(TT(t2[:, :], pC[:, :], sint[:, ch, :], OP.mult))
                    else:
                        bank["pC"] = V_(CP(t2[:, :], pC[:, :]))
                    if KROPE == 1:
                        vkr = V(TT(Tb[:, ch, :], t1[:, :], t2[:, :], OP.add))
                    else:
                        vkr = V(CP(dc if False else t1[:, :], t1[:, :])) if False else V(lambda e: e.nop())
                    if ch == 7:
                        cos_free_V = vkr
                    w("S", "V", vkr)
                w("S", "T", xsq_free[ch % 2])
                ssq = S(ACT(xsq[:, ch % 2, :], Tb[:, ch, :], AF.Square))
                w("T", "S", ssq)
                xsq_free[ch % 2] = nrm_last = T(MM(pS1[0:1, :], o128[:, 0:1], xsq[:, ch % 2, :],
                                                   ch == 0, ch == DC - 1))
                barrier(4)
            w("S", "T", nrm_last)
            sq3 = S(ACT(RR(5), pS1[0:1, :], AF.Sqrt))
            w("V", "S", sq3)
            V(RC(RR(6), RR(5)), 0)
            bank["pS1"] = vv_ = V_(TT(RR(6), RR(4), RR(6), OP.mult))
            w("T", "V", vv_[1])
            tbc, _vb = bcast(RR(6), bc1)
            vbc = ("V", _vb)
            barrier(3)
            if KPH == 22:
                return _dump_exit()
            va_prev_S = 0
            for jx in range(DC):
                jc = 32 + jx
                slot = jx % 4
                w("SY", "T", blk_use[slot])
                dblk = DS(lambda e, jc=jc, slot=slot: e.dma_start(
                    Tc[:, :, slot * 128:(slot + 1) * 128],
                    bass.AP(winall_h, jc * 128 * DC * 128, [[DC * 128, 128], [128, DC], [1, 128]])))
                pk, bn = ([pA, "pA"], [pB, "pB"])[jx % 2]
                w("T", "DS", dblk)
                wf("T", bank[bn])
                for dc in range(DC):
                    T(MM(pk[:, :], Tc[:, dc, slot * 128:(slot + 1) * 128], Ta[:, dc, :],
                         dc == 0, False), 0)
                T(MM(pk[:, :], rCB[0:1, jc * 128:(jc + 1) * 128], BB(1)[:, :], False, False), 0)
                blk_use[slot] = tvc = T(MM(pk[:, :], rCW[0:1, (jc - 16) * 128:(jc - 15) * 128],
                                           BB(2)[:, :], False, True))
                w("V", "T", tvc)
                wf("V", vbc)
                w("V", "S", va_prev_S)
                V(TT(vnb[:, :], pk[:, :], bc1[:, :], OP.mult), 0)
                bank[bn] = vu = V_(TT(t1[:, :], Tb[:, jx, :], vnb[:, :], OP.mult))
                if jx == DC - 1:
                    tb_free_V = vu[1]
                w("S", "V", vu[1])
                if jx == 0:
                    w("S", "V", ap_free_V)
                va_prev_S = S(ACT(t2[:, :], t1[:, :], AF.Identity, accum_out=Ap[:, jx:jx + 1]))
                barrier(4)
            t1_free_S = va_prev_S
            w("V", "S", va_prev_S)
            ap_free_V = V(TT(Af[:, :], Af[:, :], Ap[:, :], OP.add))
            ta_free = (tvc, ssq)
            ta_free_dom2 = "S"
            rf_free = (("V", ap_free_V), None)
            barrier(2)
            if KPH != 12:
                for k in range(16):
                    pta_step(g * 16 + k)
            if KPH == 12 and g == 0:
                return _dump_exit()

        barrier(1)
        if KPH <= 2:
            return _dump_exit()
        s_store = {0: 0, 1: 0}
        xbc_free = {0: 0, 1: 0}
        qt_free_T = {0: 0, 1: 0}
        rf_T_last = 0
        rf_V_last = ap_free_V
        drb_last = 0
        ta_xn_T = ta_free[0]
        tb_wr_guard_T = 0
        qr_done = 0
        p3_y_last_T = 0
        for g in range(G):
            gl = slice(g * NG, (g + 1) * NG)
            w("SY", "DS", rows_stored[g])
            w("SY", "T", rf_T_last)
            w("SY", "T", p3_y_last_T)
            w("SY", "V", rf_V_last)
            DS(lambda e, gl=gl: e.dma_start(RR(0), rows[0:1, gl]))
            DS(lambda e, gl=gl: e.dma_start(RR(1), rows[1:2, gl]))
            dr = DS(lambda e, gl=gl: e.dma_start(RR(2), rows[2:3, gl]))
            drb_last = DS(lambda e, gl=gl: e.dma_start(BB(0)[:, :], rowsb[0:1, gl]))
            w("G", "V", cos_free_V)
            dcs = DG(lambda e, g=g: e.dma_start(
                cost[:, :, :], bass.AP(cos_h, g * NG, [[TOK, 128], [128 * TOK, 8], [1, NG]])))
            dsn = DG(lambda e, g=g: e.dma_start(
                sint[:, :, :], bass.AP(sin_h, g * NG, [[TOK, 128], [128 * TOK, 8], [1, NG]])))
            w("V", "DS", dr)
            V(RC(RR(5), RR(1)), 0)
            V(TT(RR(3), RR(2), RR(5), OP.mult), 0)
            V(RC(RR(6), RR(2)), 0)
            v_invr = V(CP(BB(1), RR(6)))
            V(TT(RR(6), RR(0), RR(1), OP.mult), 0)
            vmb = V(CP(RR(6), RR(6)) if False else (lambda e: e.tensor_copy(RR(5), RR(6))))
            for srcrow, dstb in ((RR(1), bc1), (RR(5), bc2)):
                tb_, _ = bcast(srcrow, dstb, wv=vmb)
                rf_T_last = tb_
            vbc_all = bank["pD"][1]
            for dc in range(DC):
                xs = dc % 2
                w("G", "V", xbc_free[xs])
                dxc = DG(lambda e, dc=dc, xs=xs, g=g: e.dma_start(
                    xbc[:, xs, :], bass.AP(xh_h, dc * 128 * TOK + g * NG, [[TOK, 128], [1, NG]])))
                w("V", "DG", dxc)
                if dc == 0:
                    w("V", "T", ta_xn_T)
                    w("V", "S", t1_free_S)
                V(TT(t2[:, :], xbc[:, xs, :], bc1[:, :], OP.mult), 0)
                xbc_free[xs] = V(TT(Ta[:, dc, :], t2[:, :], bc2[:, :], OP.subtract))
            xn_done = xbc_free[(DC - 1) % 2]
            tb_, _ = bcast(RR(3), bc2)
            rf_T_last = tb_
            barrier(3)
            for jx in range(DC):
                slot = jx % 4
                w("SY", "T", blk_use[slot])
                dblk = DS(lambda e, jx=jx, slot=slot: e.dma_start(
                    Tc[:, :, slot * 128:(slot + 1) * 128],
                    bass.AP(winall_h, jx * 128 * DC * 128, [[DC * 128, 128], [128, DC], [1, 128]])))
                pk, bn = ([pA, "pA"], [pB, "pB"])[jx % 2]
                w("T", "DS", dblk)
                wf("T", bank[bn])
                if jx == 0:
                    w("T", "V", xn_done)
                    w("T", "V", v_invr)
                for dc in range(DC):
                    T(MM(pk[:, :], Tc[:, dc, slot * 128:(slot + 1) * 128], Ta[:, dc, :],
                         dc == 0, False), 0)
                blk_use[slot] = tq = T(MM(pk[:, :], rCB[0:1, jx * 128:(jx + 1) * 128],
                                          BB(1)[:, :], False, True))
                w("V", "T", tq)
                if jx == 0:
                    w("V", "T", tb_wr_guard_T)
                if jx < 8:
                    qs = jx % 2
                    w("V", "T", qt_free_T[qs])
                    w("V", "DS", s_store[qs])
                    w("V", "DG", dcs if jx == 0 else 0)
                    w("V", "DG", dsn if jx == 0 else 0)
                    bank[bn] = vq = V_(TT(qt[:, qs, :], pk[:, :], bc2[:, :], OP.mult))
                    w("T", "V", vq[1])
                    wf("T", bank["pC"])
                    qt_free_T[qs] = trh = T(MM(pC[:, :], rmts[:, :], qt[:, qs, :], True, True))
                    w("V", "T", trh)
                    V(TT(t1[:, :], qt[:, qs, :], cost[:, jx, :], OP.mult), 0)
                    bank["pC"] = V_(TT(t2[:, :], pC[:, :], sint[:, jx, :], OP.mult))
                    V(TT(Tb[:, jx, :], t1[:, :], t2[:, :], OP.add), 0)
                    qr_done = V(TSc(Tb[:, jx, :], Tb[:, jx, :], Af[:, jx:jx + 1], OP.mult))
                else:
                    bank[bn] = vq = V_(TT(t2[:, :], pk[:, :], bc2[:, :], OP.mult))
                    qr_done = V(TSc(Tb[:, jx, :], t2[:, :], Af[:, jx:jx + 1], OP.mult))
                barrier(4)
            for dco in range(DC):
                slot = dco % 4
                w("G", "T", blk_use[slot])
                dblk = DG(lambda e, dco=dco, slot=slot: e.dma_start(
                    Tc[:, :, slot * 128:(slot + 1) * 128],
                    bass.AP(woutb_h, dco * 128 * DC * 128, [[DC * 128, 128], [128, DC], [1, 128]])))
                pk, bn = ([pA, "pA"], [pB, "pB"])[dco % 2]
                w("T", "DG", dblk)
                wf("T", bank[bn])
                if dco == 0:
                    w("T", "V", qr_done)
                    w("T", "DS", drb_last)
                for dci in range(DC):
                    T(MM(pk[:, :], Tc[:, dci, slot * 128:(slot + 1) * 128], Tb[:, dci, :],
                         dci == 0, False), 0)
                blk_use[slot] = ty = T(MM(pk[:, :], boutbf[0:1, dco * 128:(dco + 1) * 128],
                                          BB(0)[:, :], False, True))
                if os.environ.get("KDBG2"):
                    w("SY", "T", ty)
                    DS(lambda e, g=g, dco=dco, slot=slot: e.dma_start(
                        dbgw_h.ap()[g, dco, :, :],
                        Tc[:, :, slot * 128:(slot + 1) * 128]))
                p3_y_last_T = ty
                w("V", "T", ty)
                ss = dco % 2
                w("V", "DS", s_store[ss])
                V(TT(t1[:, :], pk[:, :], bc1[:, :], OP.mult), 0)
                bank[bn] = vs_ = V_(TT(qt[:, ss, :], t1[:, :], Ta[:, dco, :], OP.add))
                w("SY", "V", vs_[1])
                s_store[ss] = DS(lambda e, dco=dco, g=g, ss=ss: e.dma_start(
                    sscr[dco, :, g * NG:(g + 1) * NG], qt[:, ss, :]))
                barrier(4)
            barrier(2)
            tb_wr_guard_T = p3_y_last_T
            ta_xn_T = blk_use[(DC - 1) % 4]
            rf_V_last = bank["pD"][1]
            cos_free_V = qr_done
            for k in range(16):
                ptb_step(g * 16 + k)

        barrier(1)
        if KPH <= 3:
            return _dump_exit()
        w1sl = lambda s: Ta[:, :, s * 128:(s + 1) * 128]
        wblk_T = {s: p3_y_last_T for s in range(4)}
        w2b_T = {0: p3_y_last_T, 1: p3_y_last_T}
        bank["pC"] = ("V", bank["pC"][1])
        z_rd_T = 0
        gt_free = {0: 0, 1: 0}
        out_free = {0: 0, 1: 0}
        sq_guard = 0
        out_cnt = 0
        tb_p4_T = p3_y_last_T
        rf_T_last2 = rf_T_last
        for g in range(G):
            gl = slice(g * NG, (g + 1) * NG)
            w("SY", "T", rf_T_last2)
            w("SY", "V", rf_V_last)
            DS(lambda e, gl=gl: e.dma_start(RR(0), rows[0:1, gl]))
            dr = DS(lambda e, gl=gl: e.dma_start(RR(1), rows[1:2, gl]))
            w("V", "DS", dr)
            V(TT(RR(6), RR(0), RR(1), OP.mult), 0)
            vmb = V(CP(RR(5), RR(6)))
            for srcrow, dstb in ((RR(1), bc1), (RR(5), bc2)):
                tb_, _ = bcast(srcrow, dstb, wv=vmb)
                rf_T_last2 = tb_
            rf_V_last = bank["pD"][1]
            for dc in range(DC):
                xs = dc % 2
                w("G", "V", xbc_free[xs])
                dxc = DG(lambda e, dc=dc, xs=xs, g=g: e.dma_start(
                    xbc[:, xs, :], bass.AP(xh_h, dc * 128 * TOK + g * NG, [[TOK, 128], [1, NG]])))
                w("V", "DG", dxc)
                if dc == 0:
                    w("V", "T", tb_p4_T)
                V(TT(t2[:, :], xbc[:, xs, :], bc1[:, :], OP.mult), 0)
                xbc_free[xs] = V(TT(Tb[:, dc, :], t2[:, :], bc2[:, :], OP.subtract))
            xn_done = xbc_free[(DC - 1) % 2]
            z_done = 0
            for hc in range(HC):
                s1, s3 = hc % 2, 2 + hc % 2
                w("G", "T", wblk_T[s1])
                d1 = DG(lambda e, hc=hc, s1=s1: e.dma_start(
                    w1sl(s1)[:, :, :], bass.AP(w1t_h, hc * 128 * DC * 128,
                                               [[DC * 128, 128], [128, DC], [1, 128]])))
                w("G", "T", wblk_T[s3])
                d3 = DG(lambda e, hc=hc, s3=s3: e.dma_start(
                    w1sl(s3)[:, :, :], bass.AP(w3t_h, hc * 128 * DC * 128,
                                               [[DC * 128, 128], [128, DC], [1, 128]])))
                pu, bu = ([pA, "pA"], [pB, "pB"])[hc % 2]
                pl, bl = ([pC, "pC"], [pD, "pD"])[hc % 2]
                w("T", "DG", d1)
                if hc == 0:
                    w("T", "V", xn_done)
                wf("T", bank[bu])
                for dc in range(DC):
                    tug = T(MM(pu[:, :], w1sl(s1)[:, dc, :], Tb[:, dc, :], dc == 0, dc == DC - 1),
                            1 if dc == DC - 1 else 0)
                w("T", "DG", d3)
                wf("T", bank[bl])
                for dc in range(DC):
                    tul = T(MM(pl[:, :], w1sl(s3)[:, dc, :], Tb[:, dc, :], dc == 0, dc == DC - 1),
                            1 if dc == DC - 1 else 0)
                wblk_T[s1] = wblk_T[s3] = tul
                w("S", "T", tug)
                w("S", "V", z_done)
                ssg = S_(ACT(xsq[:, 0, :], pu[:, :], AF.Silu))
                bank[bu] = ssg
                w("V", "S", ssg[1])
                w("V", "T", tul)
                if hc == 0:
                    w("V", "T", z_rd_T)
                z_done = V(TT(z[:, hc, :], xsq[:, 0, :], pl[:, :], OP.mult))
                bank[bl] = ("V", z_done)
                barrier(4)
            barrier(3)
            for dco in range(DC):
                dw2 = 0
                for h_ in range(2):
                    w("SY", "T", w2b_T[h_])
                    if g == 0 and dco == 0 and h_ == 0:
                        w("SY", "DS", PT["done"])
                    dw2 = DS(lambda e, dco=dco, h_=h_: e.dma_start(
                        Tc[:, 8 * h_:8 * h_ + 8, :],
                        bass.AP(w2h_h, dco * 128 * HC * 128 + h_ * 32 * 128,
                                [[HC * 128, 128], [128, 32], [1, 128]])))
                pf, bf_ = ([pA, "pA"], [pB, "pB"])[dco % 2]
                w("T", "DS", dw2)
                if dco == 0:
                    w("T", "V", z_done)
                wf("T", bank[bf_])
                for hc in range(HC):
                    h_, hh = hc // 32, hc % 32
                    z_rd_T = T(MM(pf[:, :], Tc[:, 8 * h_ + hh // 4, (hh % 4) * 128:(hh % 4 + 1) * 128],
                                  z[:, hc, :], hc == 0, hc == HC - 1),
                               1 if hc == HC - 1 else 0)
                w2b_T[0] = w2b_T[1] = z_rd_T
                gs = dco % 2
                w("S", "T", z_rd_T)
                w("S", "V", gt_free[0])
                sgl = S_(ACT(gt[:, :], pf[:, :], AF.Silu))
                bank[bf_] = sgl
                w("SY", "V", gt_free[gs])
                if g == 0 and dco == 0:
                    w("SY", "DS", max(s_store.values()))
                dss = DS(lambda e, dco=dco, g=g, gs=gs: e.dma_start(
                    qt[:, gs, :], sscr[dco, :, g * NG:(g + 1) * NG]))
                w("SY", "O", out_free[gs])
                dxf = DS(lambda e, dco=dco, g=g, gs=gs: e.dma_start(
                    xf[:, gs, :], bass.AP(xh_h, dco * 128 * TOK + g * NG, [[TOK, 128], [1, NG]])))
                w("V", "S", sgl[1])
                w("V", "DS", dxf)
                gt_free[0] = V(TT(t1[:, :], qt[:, gs, :], gt[:, :], OP.mult))
                gt_free[gs] = vo = V(TT(xf[:, gs, :], t1[:, :], xf[:, gs, :], OP.add))
                w("SY", "V", vo)
                out_free[gs] = out_cnt = DO(lambda e, dco=dco, g=g, gs=gs: e.dma_start(
                    y[dco * 128:(dco + 1) * 128, g * NG:(g + 1) * NG], xf[:, gs, :]))
                barrier(4)
            barrier(2)
            tb_p4_T = z_rd_T
        w("SY", "O", out_cnt)
        return c

    @block.tensor
    def _(te):
        sched("T", te)

    @block.vector
    def _(ve):
        sched("V", ve)

    @block.scalar
    def _(se):
        sched("S", se)

    @block.sync
    def _(sy):
        sched("SY", sy)

    @block.gpsimd
    def _(ge):
        sched("G", ge)

    stk.close()
    return nc


_NC_CACHE = {}


def kernel(**inputs):
    X = np.asarray(inputs["L0_attn_X"], dtype=np.float32)
    lnw = np.asarray(inputs["L0_attn_ln_w"], dtype=np.float32)
    lnb = np.asarray(inputs["L0_attn_ln_b"], dtype=np.float32)
    Win = np.ascontiguousarray(np.asarray(inputs["L0_attn_W_in"], dtype=np.float32))
    bin_ = np.asarray(inputs["L0_attn_b_in"], dtype=np.float32)
    Wout = np.asarray(inputs["L0_attn_W_out"], dtype=np.float32)
    bout = np.asarray(inputs["L0_attn_b_out"], dtype=np.float32)
    W1 = np.asarray(inputs["L0_ffn_W1"], dtype=np.float32)
    W2 = np.asarray(inputs["L0_ffn_W2"], dtype=np.float32)
    W3 = np.asarray(inputs["L0_ffn_W3"], dtype=np.float32)

    cosfm, sinfm, h128, h64b, rmt = _consts()
    woutb = np.ascontiguousarray(Wout.reshape(DC, 128, DC, 128).transpose(2, 1, 0, 3))
    w1t = np.ascontiguousarray(W1.reshape(HC, 128, DC, 128).transpose(0, 3, 2, 1))
    w3t = np.ascontiguousarray(W3.reshape(HC, 128, DC, 128).transpose(0, 3, 2, 1))
    w2t = np.ascontiguousarray(W2.T)
    lnwb = np.ascontiguousarray(np.stack([lnb.reshape(DC, 128).T, lnw.reshape(DC, 128).T], axis=-1))
    shared = dict(
        win=Win, woutb=woutb, w1t=w1t, w3t=w3t, w2t=w2t,
        cosfm=cosfm, sinfm=sinfm, lnwb=lnwb,
        binrow=bin_[None, :].copy(), boutrow=bout[None, :].copy(),
        h128c=h128, h64bc=h64b, rmtc=rmt,
        on128=np.ones((128, 1), dtype=ml_dtypes.bfloat16),
        onrow=np.ones((1, 128), dtype=np.float32),
    )
    in_maps = []
    for b in range(8):
        m = dict(shared)
        m["xh"] = np.ascontiguousarray(X[b].T)
        in_maps.append(m)
    if "nc" not in _NC_CACHE:
        _NC_CACHE["nc"] = build_nc()
    res = run_bass_kernel_spmd(_NC_CACHE["nc"], in_maps, core_ids=list(range(8)))
    out = np.empty((8, TOK, D), dtype=np.float32)
    for b in range(8):
        out[b] = np.asarray(res.results[b]["y"]).T
    return out
